# version 01 (final)
# baseline (speedup 1.0000x reference)
import numpy as np
import ml_dtypes
from contextlib import ExitStack
import concourse.bass as bass
import concourse.mybir as mybir
from concourse.bass_utils import run_bass_kernel_spmd

F32 = mybir.dt.float32
BF16 = mybir.dt.bfloat16
I32 = mybir.dt.int32
AF = mybir.ActivationFunctionType
OP = mybir.AluOpType
AX = mybir.AxisListType.X
BF = ml_dtypes.bfloat16
NCORE = 8


class Seq:

    def __init__(self):
        self.steps = []

    def s(self, eng, emit, dma=False):
        self.steps.append((eng, emit, dma))

    def dma(self, out, in_):
        self.s("sync", lambda e, o=out, i=in_: e.dma_start(out=o, in_=i), dma=True)

    def run(self, nc, st):
        S = st.enter_context(nc.semaphore("S"))
        D = st.enter_context(nc.semaphore("D"))
        sched = {"sync": [], "tensor": [], "scalar": [], "vector": [], "gpsimd": []}
        totS, totD = 0, 0
        prev_eng = None
        last = {k: [-1, -1] for k in sched}
        for (eng, emit, dma) in self.steps:
            skip = (prev_eng == eng) and (eng == "tensor" or dma)
            wS = wD = None
            if not skip:
                if totS > last[eng][0]:
                    wS = totS
                    last[eng][0] = totS
                if totD > last[eng][1]:
                    wD = totD
                    last[eng][1] = totD
            sched[eng].append((wS, wD, emit, dma))
            if dma:
                totD += 16
            else:
                totS += 1
            prev_eng = eng
        sched["sync"].append((totS, totD, None, False))
        blk = st.enter_context(nc.Block())
        for name, entries in sched.items():
            if not entries:
                continue

            def mk(entries):
                def body(eng):
                    for (wS, wD, emit, dma) in entries:
                        if wS is not None:
                            eng.wait_ge(S, wS)
                        if wD is not None:
                            eng.wait_ge(D, wD)
                        if emit is None:
                            continue
                        inst = emit(eng)
                        if dma:
                            inst.then_inc(D, 16)
                        else:
                            inst.then_inc(S, 1)
                return body

            getattr(blk, name)(mk(entries))


def _ln_steps(q, xt, tmp, d, eps, out_emit):
    m, c1, sq, ep = tmp["m"], tmp["c1"], tmp["sq"], tmp["ep"]
    q.s("vector", lambda e: e.reduce_sum(m[:, 0:1], xt[:, :d], AX))
    q.s("scalar", lambda e: e.mul(c1[:, 0:1], m[:, 0:1], 1.0 / d))
    q.s("vector", lambda e: e.tensor_scalar(xt[:, :d], xt[:, :d], c1[:, 0:1], None, OP.subtract))
    q.s("scalar", lambda e: e.activation(sq[:, :d], xt[:, :d], AF.Square))
    q.s("vector", lambda e: e.reduce_sum(m[:, 0:1], sq[:, :d], AX))
    q.s("scalar", lambda e: e.activation(c1[:, 0:1], m[:, 0:1], AF.Ln, bias=ep[:, 0:1], scale=1.0 / d))
    q.s("scalar", lambda e: e.activation(c1[:, 0:1], c1[:, 0:1], AF.Exp, scale=-0.5))
    out_emit(c1)


def build_ln(tokens, d, eps=1e-5):
    nc = bass.Bass("TRN2", target_bir_lowering=False)
    x = nc.dram_tensor("x", [tokens, d], F32, kind="ExternalInput").ap()
    y = nc.dram_tensor("y", [tokens, d], BF16, kind="ExternalOutput").ap()
    st = ExitStack()
    with st:
        xt = st.enter_context(nc.sbuf_tensor("xt", [128, d], F32))
        sq = st.enter_context(nc.sbuf_tensor("sq", [128, d], F32))
        ot = st.enter_context(nc.sbuf_tensor("ot", [128, d], BF16))
        m = st.enter_context(nc.sbuf_tensor("m", [128, 1], F32))
        c1 = st.enter_context(nc.sbuf_tensor("c1", [128, 1], F32))
        ep = st.enter_context(nc.sbuf_tensor("ep", [128, 1], F32))
        q = Seq()
        q.s("vector", lambda e: e.memset(ep[:, :], float(eps)))
        tmp = {"m": m, "c1": c1, "sq": sq, "ep": ep}
        for t in range(tokens // 128):
            q.dma(xt[:, :], x[t * 128:(t + 1) * 128, :])

            def fin(c1, t=t):
                q.s("vector", lambda e: e.tensor_scalar(ot[:, :], xt[:, :], c1[:, 0:1], None, OP.mult))
                q.dma(y[t * 128:(t + 1) * 128, :], ot[:, :])
            _ln_steps(q, xt, tmp, d, eps, fin)
        q.run(nc, st)
    return nc


def build_mm_v1(K, M, n_sizes, w_names, out_name, out_N, dual=False, res=False,
                a_bf=True, out_dtype=F32, w_bf=False):
    nc = bass.Bass("TRN2", target_bir_lowering=False)
    at = nc.dram_tensor("at", [K, M], BF16 if a_bf else F32, kind="ExternalInput").ap()
    ws = [nc.dram_tensor(w, [K, out_N], BF16 if w_bf else F32, kind="ExternalInput").ap() for w in w_names]
    if res:
        r = nc.dram_tensor("res", [M, out_N], F32, kind="ExternalInput").ap()
    o = nc.dram_tensor(out_name, [M, out_N], out_dtype, kind="ExternalOutput").ap()
    nK = K // 128
    st = ExitStack()
    with st:
        asb = st.enter_context(nc.sbuf_tensor("asb", [128, nK * M], BF16))
        wst = st.enter_context(nc.sbuf_tensor("wst", [128, 512], F32))
        wsb = [st.enter_context(nc.sbuf_tensor(f"wsb{i}", [128, nK * 512], BF16)) for i in range(len(ws))]
        ps = [st.enter_context(nc.psum_tensor(f"ps{i}", [128, 512], F32)) for i in range(len(ws))]
        sg = st.enter_context(nc.sbuf_tensor("sg", [128, 512], F32))
        rt = st.enter_context(nc.sbuf_tensor("rt", [128, 512], F32))
        ot = st.enter_context(nc.sbuf_tensor("ot", [128, 512], out_dtype))
        q = Seq()
        if a_bf:
            for kt in range(nK):
                q.dma(asb[:, kt * M:(kt + 1) * M], at[kt * 128:(kt + 1) * 128, :])
        else:
            ast = st.enter_context(nc.sbuf_tensor("ast", [128, M], F32))
            for kt in range(nK):
                q.dma(ast[:, :], at[kt * 128:(kt + 1) * 128, :])
                q.s("scalar", lambda e, kt=kt: e.copy(asb[:, kt * M:(kt + 1) * M], ast[:, :]))
        n0 = 0
        for nw in n_sizes:
            for wi, w in enumerate(ws):
                for kt in range(nK):
                    if w_bf:
                        q.dma(wsb[wi][:, kt * 512:kt * 512 + nw], w[kt * 128:(kt + 1) * 128, n0:n0 + nw])
                    else:
                        q.dma(wst[:, :nw], w[kt * 128:(kt + 1) * 128, n0:n0 + nw])
                        q.s("scalar", lambda e, wi=wi, kt=kt, nw=nw: e.copy(wsb[wi][:, kt * 512:kt * 512 + nw], wst[:, :nw]))
            for mt in range(M // 128):
                for wi in range(len(ws)):
                    for kt in range(nK):
                        q.s("tensor", lambda e, wi=wi, kt=kt, mt=mt, nw=nw: e.matmul(
                            ps[wi][:, :nw], asb[:, kt * M + mt * 128:kt * M + mt * 128 + 128],
                            wsb[wi][:, kt * 512:kt * 512 + nw], start=(kt == 0), stop=(kt == nK - 1)))
                if dual:
                    q.s("scalar", lambda e, nw=nw: e.activation(sg[:, :nw], ps[0][:, :nw], AF.Silu))
                    q.s("vector", lambda e, nw=nw: e.tensor_tensor(ot[:, :nw], sg[:, :nw], ps[1][:, :nw], OP.mult))
                elif res:
                    q.dma(rt[:, :nw], r[mt * 128:(mt + 1) * 128, n0:n0 + nw])
                    q.s("vector", lambda e, nw=nw: e.tensor_tensor(ot[:, :nw], rt[:, :nw], ps[0][:, :nw], OP.add))
                else:
                    q.s("scalar", lambda e, nw=nw: e.copy(ot[:, :nw], ps[0][:, :nw]))
                q.dma(o[mt * 128:(mt + 1) * 128, n0:n0 + nw], ot[:, :nw])
            n0 += nw
        q.run(nc, st)
    return nc


def build_mm_v2(K, M, N, out_name, ln_name, a_bf=True, w_bf=False, eps=1e-5):
    nc = bass.Bass("TRN2", target_bir_lowering=False)
    at = nc.dram_tensor("at", [K, M], BF16 if a_bf else F32, kind="ExternalInput").ap()
    w = nc.dram_tensor("w", [K, N], BF16 if w_bf else F32, kind="ExternalInput").ap()
    r = nc.dram_tensor("res", [M, N], F32, kind="ExternalInput").ap()
    o = nc.dram_tensor(out_name, [M, N], F32, kind="ExternalOutput").ap()
    ol = nc.dram_tensor(ln_name, [M, N], BF16, kind="ExternalOutput").ap()
    nK = K // 128
    nN = N // 512
    st = ExitStack()
    with st:
        asb = st.enter_context(nc.sbuf_tensor("asb", [128, nK * M], BF16))
        wst = st.enter_context(nc.sbuf_tensor("wst", [128, 512], F32))
        wsb = st.enter_context(nc.sbuf_tensor("wsb", [128, 512], BF16))
        ps = st.enter_context(nc.psum_tensor("ps", [128, 512], F32))
        rt = st.enter_context(nc.sbuf_tensor("rt", [128, 512], F32))
        xt = st.enter_context(nc.sbuf_tensor("xt", [128, N], F32))
        sq = st.enter_context(nc.sbuf_tensor("sq", [128, N], F32))
        lt = st.enter_context(nc.sbuf_tensor("lt", [128, N], BF16))
        m = st.enter_context(nc.sbuf_tensor("m", [128, 1], F32))
        c1 = st.enter_context(nc.sbuf_tensor("c1", [128, 1], F32))
        ep = st.enter_context(nc.sbuf_tensor("ep", [128, 1], F32))
        q = Seq()
        q.s("vector", lambda e: e.memset(ep[:, :], float(eps)))
        if a_bf:
            for kt in range(nK):
                q.dma(asb[:, kt * M:(kt + 1) * M], at[kt * 128:(kt + 1) * 128, :])
        else:
            ast = st.enter_context(nc.sbuf_tensor("ast", [128, M], F32))
            for kt in range(nK):
                q.dma(ast[:, :], at[kt * 128:(kt + 1) * 128, :])
                q.s("scalar", lambda e, kt=kt: e.copy(asb[:, kt * M:(kt + 1) * M], ast[:, :]))
        tmp = {"m": m, "c1": c1, "sq": sq, "ep": ep}
        for mt in range(M // 128):
            for nt in range(nN):
                for kt in range(nK):
                    if w_bf:
                        q.dma(wsb[:, :], w[kt * 128:(kt + 1) * 128, nt * 512:(nt + 1) * 512])
                    else:
                        q.dma(wst[:, :], w[kt * 128:(kt + 1) * 128, nt * 512:(nt + 1) * 512])
                        q.s("scalar", lambda e: e.copy(wsb[:, :], wst[:, :]))
                    q.s("tensor", lambda e, kt=kt, mt=mt: e.matmul(
                        ps[:, :], asb[:, kt * M + mt * 128:kt * M + mt * 128 + 128],
                        wsb[:, :], start=(kt == 0), stop=(kt == nK - 1)))
                q.dma(rt[:, :], r[mt * 128:(mt + 1) * 128, nt * 512:(nt + 1) * 512])
                q.s("vector", lambda e, nt=nt: e.tensor_tensor(xt[:, nt * 512:(nt + 1) * 512], rt[:, :], ps[:, :], OP.add))
            q.dma(o[mt * 128:(mt + 1) * 128, :], xt[:, :])

            def fin(c1, mt=mt):
                q.s("vector", lambda e: e.tensor_scalar(lt[:, :], xt[:, :], c1[:, 0:1], None, OP.mult))
                q.dma(ol[mt * 128:(mt + 1) * 128, :], lt[:, :])
            _ln_steps(q, xt, tmp, N, eps, fin)
        q.run(nc, st)
    return nc


def build_conv(nch, T, S):
    nc = bass.Bass("TRN2", target_bir_lowering=False)
    x = nc.dram_tensor("x", [nch, T], F32, kind="ExternalInput").ap()
    cw = nc.dram_tensor("cw", [nch, 4], F32, kind="ExternalInput").ap()
    cb = nc.dram_tensor("cb", [nch, 1], F32, kind="ExternalInput").ap()
    y = nc.dram_tensor("y", [nch, T], F32, kind="ExternalOutput").ap()
    st = ExitStack()
    with st:
        xt = st.enter_context(nc.sbuf_tensor("xt", [128, T], F32))
        acc = st.enter_context(nc.sbuf_tensor("acc", [128, T], F32))
        yt = st.enter_context(nc.sbuf_tensor("yt", [128, T], F32))
        wt = st.enter_context(nc.sbuf_tensor("wt", [128, 4], F32))
        bt = st.enter_context(nc.sbuf_tensor("bt", [128, 1], F32))
        q = Seq()
        for ct in range(nch // 128):
            r0 = ct * 128
            q.dma(xt[:, :], x[r0:r0 + 128, :])
            q.dma(wt[:, :], cw[r0:r0 + 128, :])
            q.dma(bt[:, :], cb[r0:r0 + 128, :])
            for b in range(T // S):
                o0 = b * S
                q.s("vector", lambda e, o0=o0: e.tensor_scalar(
                    acc[:, o0:o0 + S], xt[:, o0:o0 + S], wt[:, 3:4], None, OP.mult))
                for k in (2, 1, 0):
                    sh = 3 - k
                    q.s("vector", lambda e, o0=o0, k=k, sh=sh: e.scalar_tensor_tensor(
                        acc[:, o0 + sh:o0 + S], xt[:, o0:o0 + S - sh], wt[:, k:k + 1],
                        acc[:, o0 + sh:o0 + S], OP.mult, OP.add))
            q.s("scalar", lambda e: e.activation(yt[:, :], acc[:, :], AF.Silu, bias=bt[:, 0:1]))
            q.dma(y[r0:r0 + 128, :], yt[:, :])
        q.run(nc, st)
    return nc


def build_dtprep():
    nc = bass.Bass("TRN2", target_bir_lowering=False)
    dtr = nc.dram_tensor("dtr", [128, 256], F32, kind="ExternalInput").ap()
    dtb = nc.dram_tensor("dtb", [128, 1], F32, kind="ExternalInput").ap()
    alog = nc.dram_tensor("alog", [128, 1], F32, kind="ExternalInput").ap()
    dt2 = nc.dram_tensor("dt2", [128, 256], F32, kind="ExternalOutput").ap()
    acum = nc.dram_tensor("acum", [128, 256], F32, kind="ExternalOutput").ap()
    st = ExitStack()
    with st:
        xt = st.enter_context(nc.sbuf_tensor("xt", [128, 256], F32))
        ab = st.enter_context(nc.sbuf_tensor("ab", [128, 256], F32))
        l = st.enter_context(nc.sbuf_tensor("l", [128, 256], F32))
        rr = st.enter_context(nc.sbuf_tensor("rr", [128, 256], F32))
        d2 = st.enter_context(nc.sbuf_tensor("d2", [128, 256], F32))
        cA = st.enter_context(nc.sbuf_tensor("cA", [128, 256], F32))
        cB = st.enter_context(nc.sbuf_tensor("cB", [128, 256], F32))
        bcol = st.enter_context(nc.sbuf_tensor("bcol", [128, 1], F32))
        acol = st.enter_context(nc.sbuf_tensor("acol", [128, 1], F32))
        q = Seq()
        q.dma(xt[:, :], dtr[:, :])
        q.dma(bcol[:, :], dtb[:, :])
        q.dma(acol[:, :], alog[:, :])
        q.s("vector", lambda e: e.tensor_scalar(xt[:, :], xt[:, :], bcol[:, 0:1], None, OP.add))
        q.s("scalar", lambda e: e.activation(ab[:, :], xt[:, :], AF.Abs))
        q.s("scalar", lambda e: e.activation(ab[:, :], ab[:, :], AF.Exp, scale=-1.0))
        q.s("scalar", lambda e: e.activation(l[:, :], ab[:, :], AF.Ln, bias=1.0))
        q.s("vector", lambda e: e.tensor_scalar(rr[:, :], xt[:, :], 0.0, None, OP.max))
        q.s("vector", lambda e: e.tensor_add(d2[:, :], rr[:, :], l[:, :]))
        q.dma(dt2[:, :], d2[:, :])
        q.s("scalar", lambda e: e.activation(acol[:, :], acol[:, :], AF.Exp))
        q.s("vector", lambda e: e.tensor_scalar(cA[:, :], d2[:, :], acol[:, 0:1], -1.0, OP.mult, OP.mult))
        cur, nxt = cA, cB
        sh = 1
        while sh < 256:
            q.s("vector", lambda e, cur=cur, nxt=nxt, sh=sh: e.tensor_copy(nxt[:, 0:sh], cur[:, 0:sh]))
            q.s("vector", lambda e, cur=cur, nxt=nxt, sh=sh: e.tensor_add(
                nxt[:, sh:256], cur[:, sh:256], cur[:, 0:256 - sh]))
            cur, nxt = nxt, cur
            sh *= 2
        q.dma(acum[:, :], cur[:, :])
        q.run(nc, st)
    return nc


def build_scan():
    nc = bass.Bass("TRN2", target_bir_lowering=False)
    hsx = nc.dram_tensor("hsx", [32768, 128], F32, kind="ExternalInput").ap()
    bln = nc.dram_tensor("bln", [32768, 128], F32, kind="ExternalInput").ap()
    btr = nc.dram_tensor("btr", [1024, 4096], F32, kind="ExternalInput").ap()
    ctr = nc.dram_tensor("ctr", [1024, 4096], F32, kind="ExternalInput").ap()
    acT = nc.dram_tensor("acT", [1024, 32], F32, kind="ExternalInput").ap()
    dtT = nc.dram_tensor("dtT", [1024, 32], F32, kind="ExternalInput").ap()
    abt = nc.dram_tensor("abt", [1024, 16], F32, kind="ExternalInput").ap()
    dcol = nc.dram_tensor("dcol", [1024, 1], F32, kind="ExternalInput").ap()
    arows = nc.dram_tensor("arows", [128, 256], F32, kind="ExternalInput").ap()
    mut = nc.dram_tensor("mut", [256, 256], F32, kind="ExternalInput").ap()
    ones1 = nc.dram_tensor("ones1", [1, 128], F32, kind="ExternalInput").ap()
    yo = nc.dram_tensor("yout", [32768, 128], F32, kind="ExternalOutput").ap()
    st = ExitStack()
    with st:
        sb = lambda n, p, f, d=F32: st.enter_context(nc.sbuf_tensor(n, [p, f], d))
        acTt = sb("acTt", 128, 32)
        dtTt = sb("dtTt", 128, 32)
        abtt = sb("abtt", 128, 16)
        cdt = sb("cdt", 128, 16)
        sdt = sb("sdt", 128, 32)
        dct = sb("dct", 128, 1)
        on1 = sb("on1", 1, 128)
        arw = sb("arw", 1, 256)
        mk = [sb(f"mk{i}", 128, 256) for i in range(2)]
        bts = sb("bts", 128, 256)
        btb = sb("btb", 128, 256, BF16)
        cts = sb("cts", 128, 256)
        ctb = sb("ctb", 128, 256, BF16)
        hs = [sb(f"hs{i}", 128, 128) for i in range(2)]
        bn = [sb(f"bn{i}", 128, 128) for i in range(2)]
        xbf = [sb(f"xbf{i}", 128, 128, BF16) for i in range(2)]
        bsc = [sb(f"bsc{i}", 128, 128, BF16) for i in range(2)]
        dst = sb("dst", 128, 1)
        dse = sb("dse", 128, 1)
        dts = sb("dts", 128, 256)
        lts = sb("lts", 128, 256)
        ltm = sb("ltm", 128, 256)
        et = [sb(f"et{i}", 128, 256, BF16) for i in range(2)]
        t1s = sb("t1s", 128, 128)
        t2s = sb("t2s", 128, 128)
        yts = sb("yts", 128, 128)
        Sa = sb("Sa", 128, 128)
        Sb_ = sb("Sb", 128, 128)
        Sbf = sb("Sbf", 128, 128, BF16)
        rowb = st.enter_context(nc.psum_tensor("rowb", [128, 256], F32))
        g1 = st.enter_context(nc.psum_tensor("g1", [128, 256], F32))
        yd = st.enter_context(nc.psum_tensor("yd", [128, 128], F32))
        yop = st.enter_context(nc.psum_tensor("yop", [128, 128], F32))
        stp = st.enter_context(nc.psum_tensor("stp", [128, 128], F32))
        q = Seq()
        q.dma(on1[:, :], ones1[:, :])
        q.dma(mk[0][:, :], mut[0:128, :])
        q.dma(mk[1][:, :], mut[128:256, :])
        for bh in range(8):
            rb = bh * 128
            q.dma(acTt[:, :], acT[rb:rb + 128, :])
            q.dma(dtTt[:, :], dtT[rb:rb + 128, :])
            q.dma(abtt[:, :], abt[rb:rb + 128, :])
            q.dma(dct[:, :], dcol[rb:rb + 128, :])
            q.s("scalar", lambda e: e.activation(cdt[:, :], abtt[:, :], AF.Exp))
            q.s("scalar", lambda e: e.activation(sdt[:, :], acTt[:, :], AF.Exp))
            q.s("vector", lambda e: e.memset(Sa[:, :], 0.0))
            q.s("vector", lambda e: e.memset(Sbf[:, :], 0.0))
            Scur, Snxt = Sa, Sb_
            for c in range(16):
                s0 = bh * 4096 + c * 256
                q.dma(bts[:, :], btr[rb:rb + 128, c * 256:(c + 1) * 256])
                q.s("scalar", lambda e: e.copy(btb[:, :], bts[:, :]))
                q.dma(cts[:, :], ctr[rb:rb + 128, c * 256:(c + 1) * 256])
                q.s("scalar", lambda e: e.copy(ctb[:, :], cts[:, :]))
                q.dma(hs[0][:, :], hsx[s0:s0 + 128, :])
                q.dma(hs[1][:, :], hsx[s0 + 128:s0 + 256, :])
                q.dma(bn[0][:, :], bln[s0:s0 + 128, :])
                q.dma(bn[1][:, :], bln[s0 + 128:s0 + 256, :])
                q.dma(arw[:, :], arows[bh * 16 + c:bh * 16 + c + 1, :])
                q.s("tensor", lambda e: e.matmul(rowb[:, :], on1[:, :], arw[:, :], start=True, stop=True))
                for h in range(2):
                    j = 2 * c + h
                    q.s("vector", lambda e, h=h, j=j: e.tensor_scalar(
                        xbf[h][:, :], hs[h][:, :], dtTt[:, j:j + 1], None, OP.mult))
                    q.s("vector", lambda e, c=c, j=j: e.tensor_sub(dst[:, :], abtt[:, c:c + 1], acTt[:, j:j + 1]))
                    q.s("scalar", lambda e: e.activation(dse[:, :], dst[:, :], AF.Exp))
                    q.s("vector", lambda e, h=h: e.tensor_scalar(
                        bsc[h][:, :], bn[h][:, :], dse[:, 0:1], None, OP.mult))
                for sh_ in range(2):
                    jS = 2 * c + sh_
                    q.s("vector", lambda e, jS=jS: e.tensor_scalar(
                        dts[:, :], rowb[:, :], acTt[:, jS:jS + 1], 0.0, OP.subtract, OP.min))
                    q.s("scalar", lambda e: e.activation(lts[:, :], dts[:, :], AF.Exp))
                    q.s("vector", lambda e, sh_=sh_: e.tensor_mul(ltm[:, :], lts[:, :], mk[sh_][:, :]))
                    q.s("tensor", lambda e, sh_=sh_: e.matmul(
                        g1[:, :], btb[:, sh_ * 128:(sh_ + 1) * 128], ctb[:, :], start=True, stop=True))
                    q.s("vector", lambda e, sh_=sh_: e.tensor_tensor(et[sh_][:, :], g1[:, :], ltm[:, :], OP.mult))
                for h in range(2):
                    j = 2 * c + h
                    q.s("tensor", lambda e, h=h: e.matmul(
                        yd[:, :], et[0][:, h * 128:(h + 1) * 128], xbf[0][:, :], start=True, stop=False))
                    q.s("tensor", lambda e, h=h: e.matmul(
                        yd[:, :], et[1][:, h * 128:(h + 1) * 128], xbf[1][:, :], start=False, stop=True))
                    q.s("tensor", lambda e, h=h: e.matmul(
                        yop[:, :], ctb[:, h * 128:(h + 1) * 128], Sbf[:, :], start=True, stop=True))
                    q.s("vector", lambda e, j=j: e.tensor_scalar(
                        t1s[:, :], yop[:, :], sdt[:, j:j + 1], None, OP.mult))
                    q.s("vector", lambda e, h=h: e.scalar_tensor_tensor(
                        t2s[:, :], hs[h][:, :], dct[:, 0:1], yd[:, :], OP.mult, OP.add))
                    q.s("vector", lambda e: e.tensor_add(yts[:, :], t1s[:, :], t2s[:, :]))
                    q.dma(yo[s0 + h * 128:s0 + (h + 1) * 128, :], yts[:, :])
                q.s("tensor", lambda e: e.matmul(stp[:, :], bsc[0][:, :], xbf[0][:, :], start=True, stop=False))
                q.s("tensor", lambda e: e.matmul(stp[:, :], bsc[1][:, :], xbf[1][:, :], start=False, stop=True))
                q.s("vector", lambda e, Scur=Scur, Snxt=Snxt, c=c: e.scalar_tensor_tensor(
                    Snxt[:, :], Scur[:, :], cdt[:, c:c + 1], stp[:, :], OP.mult, OP.add))
                q.s("scalar", lambda e, Snxt=Snxt: e.copy(Sbf[:, :], Snxt[:, :]))
                Scur, Snxt = Snxt, Scur
        q.run(nc, st)
    return nc


def build_gating(tokens, I, gs, eps=1e-6):
    nc = bass.Bass("TRN2", target_bir_lowering=False)
    yy = nc.dram_tensor("yy", [tokens, I], F32, kind="ExternalInput").ap()
    gg = nc.dram_tensor("gg", [tokens, I], F32, kind="ExternalInput").ap()
    nw = nc.dram_tensor("nw", [128, I], F32, kind="ExternalInput").ap()
    hh = nc.dram_tensor("hh", [tokens, I], BF16, kind="ExternalOutput").ap()
    ng = I // gs
    st = ExitStack()
    with st:
        yt = st.enter_context(nc.sbuf_tensor("yt", [128, I], F32))
        gt = st.enter_context(nc.sbuf_tensor("gt", [128, I], F32))
        nwt = st.enter_context(nc.sbuf_tensor("nwt", [128, I], F32))
        sgt = st.enter_context(nc.sbuf_tensor("sgt", [128, I], F32))
        ht = st.enter_context(nc.sbuf_tensor("ht", [128, I], F32))
        sqt = st.enter_context(nc.sbuf_tensor("sqt", [128, I], F32))
        hnt = st.enter_context(nc.sbuf_tensor("hnt", [128, I], F32))
        ot = st.enter_context(nc.sbuf_tensor("ot", [128, I], BF16))
        va = st.enter_context(nc.sbuf_tensor("va", [128, ng], F32))
        rs = st.enter_context(nc.sbuf_tensor("rs", [128, ng], F32))
        ep = st.enter_context(nc.sbuf_tensor("ep", [128, 1], F32))
        q = Seq()
        q.s("vector", lambda e: e.memset(ep[:, :], float(eps)))
        q.dma(nwt[:, :], nw[:, :])
        for t in range(tokens // 128):
            q.dma(yt[:, :], yy[t * 128:(t + 1) * 128, :])
            q.dma(gt[:, :], gg[t * 128:(t + 1) * 128, :])
            q.s("scalar", lambda e: e.activation(sgt[:, :], gt[:, :], AF.Silu))
            q.s("vector", lambda e: e.tensor_mul(ht[:, :], yt[:, :], sgt[:, :]))
            q.s("scalar", lambda e: e.activation(sqt[:, :], ht[:, :], AF.Square))
            for g in range(ng):
                q.s("vector", lambda e, g=g: e.reduce_sum(va[:, g:g + 1], sqt[:, g * gs:(g + 1) * gs], AX))
            q.s("scalar", lambda e: e.activation(rs[:, :], va[:, :], AF.Ln, bias=ep[:, 0:1], scale=1.0 / gs))
            q.s("scalar", lambda e: e.activation(rs[:, :], rs[:, :], AF.Exp, scale=-0.5))
            for g in range(ng):
                q.s("vector", lambda e, g=g: e.tensor_scalar(
                    hnt[:, g * gs:(g + 1) * gs], ht[:, g * gs:(g + 1) * gs], rs[:, g:g + 1], None, OP.mult))
            q.s("vector", lambda e: e.tensor_mul(ot[:, :], hnt[:, :], nwt[:, :]))
            q.dma(hh[t * 128:(t + 1) * 128, :], ot[:, :])
        q.run(nc, st)
    return nc


def build_ffn0(M=8192, K=2048, NS=688, ND=256):
    nc = bass.Bass("TRN2", target_bir_lowering=False)
    at = nc.dram_tensor("at", [K, M], BF16, kind="ExternalInput").ap()
    gix = nc.dram_tensor("gix", [K, NS], I32, kind="ExternalInput").ap()
    uix = nc.dram_tensor("uix", [K, NS], I32, kind="ExternalInput").ap()
    dix = nc.dram_tensor("dix", [5504, ND], I32, kind="ExternalInput").ap()
    lut = nc.dram_tensor("lut", [128, 16], F32, kind="ExternalInput").ap()
    hf = nc.dram_tensor("hf", [M, NS], BF16, kind="ExternalOutput").ap()
    wd = nc.dram_tensor("wd", [5504, ND], BF16, kind="ExternalOutput").ap()
    nK = K // 128
    nsz = [512, NS - 512]
    st = ExitStack()
    with st:
        lutt = st.enter_context(nc.sbuf_tensor("lutt", [128, 16], F32))
        it = st.enter_context(nc.sbuf_tensor("it", [128, 512], I32))
        idxf = st.enter_context(nc.sbuf_tensor("idxf", [128, 512], F32))
        tmpv = st.enter_context(nc.sbuf_tensor("tmpv", [128, 512], F32))
        accA = st.enter_context(nc.sbuf_tensor("accA", [128, 512], F32))
        accB = st.enter_context(nc.sbuf_tensor("accB", [128, 512], F32))
        wg = st.enter_context(nc.sbuf_tensor("wg", [128, nK * NS], BF16))
        wu = st.enter_context(nc.sbuf_tensor("wu", [128, nK * NS], BF16))
        wdb = st.enter_context(nc.sbuf_tensor("wdb", [128, ND], BF16))
        asb = st.enter_context(nc.sbuf_tensor("asb2", [128, K], BF16))
        sg = st.enter_context(nc.sbuf_tensor("sg", [128, 512], F32))
        hbt = st.enter_context(nc.sbuf_tensor("hbt", [128, 512], BF16))
        psG = st.enter_context(nc.psum_tensor("psG", [128, 512], F32))
        psU = st.enter_context(nc.psum_tensor("psU", [128, 512], F32))
        q = Seq()
        q.dma(lutt[:, :], lut[:, :])

        def gather(src_ap, r0, c0, nw, dst_emit):
            q.dma(it[:, :nw], src_ap[r0:r0 + 128, c0:c0 + nw])
            q.s("vector", lambda e: e.tensor_copy(idxf[:, :nw], it[:, :nw]))
            q.s("vector", lambda e: e.tensor_scalar(
                accA[:, :nw], idxf[:, :nw], 0.0, lutt[:, 0:1], OP.is_equal, OP.mult))
            cur, nxt = accA, accB
            for v in range(1, 16):
                q.s("vector", lambda e, v=v: e.tensor_scalar(
                    tmpv[:, :nw], idxf[:, :nw], float(v), lutt[:, v:v + 1], OP.is_equal, OP.mult))
                q.s("vector", lambda e, cur=cur, nxt=nxt: e.tensor_add(nxt[:, :nw], cur[:, :nw], tmpv[:, :nw]))
                cur, nxt = nxt, cur
            dst_emit(cur, nw)

        for wsb, src in ((wg, gix), (wu, uix)):
            for kt in range(nK):
                c0 = 0
                for nw in nsz:
                    def emit(cur, nw, wsb=wsb, kt=kt, c0=c0):
                        q.s("scalar", lambda e: e.copy(wsb[:, kt * NS + c0:kt * NS + c0 + nw], cur[:, :nw]))
                    gather(src, kt * 128, c0, nw, emit)
                    c0 += nw
        for kt in range(5504 // 128):
            def emitd(cur, nw, kt=kt):
                q.s("scalar", lambda e: e.copy(wdb[:, :nw], cur[:, :nw]))
                q.dma(wd[kt * 128:(kt + 1) * 128, :nw], wdb[:, :nw])
            gather(dix, kt * 128, 0, ND, emitd)
        for mt in range(M // 128):
            for kt in range(nK):
                q.dma(asb[:, kt * 128:(kt + 1) * 128], at[kt * 128:(kt + 1) * 128, mt * 128:(mt + 1) * 128])
            c0 = 0
            for nw in nsz:
                for kt in range(nK):
                    q.s("tensor", lambda e, kt=kt, c0=c0, nw=nw: e.matmul(
                        psG[:, :nw], asb[:, kt * 128:(kt + 1) * 128],
                        wg[:, kt * NS + c0:kt * NS + c0 + nw], start=(kt == 0), stop=(kt == nK - 1)))
                for kt in range(nK):
                    q.s("tensor", lambda e, kt=kt, c0=c0, nw=nw: e.matmul(
                        psU[:, :nw], asb[:, kt * 128:(kt + 1) * 128],
                        wu[:, kt * NS + c0:kt * NS + c0 + nw], start=(kt == 0), stop=(kt == nK - 1)))
                q.s("scalar", lambda e, nw=nw: e.activation(sg[:, :nw], psG[:, :nw], AF.Silu))
                q.s("vector", lambda e, nw=nw: e.tensor_tensor(hbt[:, :nw], sg[:, :nw], psU[:, :nw], OP.mult))
                q.dma(hf[mt * 128:(mt + 1) * 128, c0:c0 + nw], hbt[:, :nw])
                c0 += nw
        q.run(nc, st)
    return nc


def build_qkv(M=1024, K=2048, N=6144):
    nc = bass.Bass("TRN2", target_bir_lowering=False)
    at = nc.dram_tensor("at", [K, M], BF16, kind="ExternalInput").ap()
    w = nc.dram_tensor("w", [K, N], F32, kind="ExternalInput").ap()
    cs = nc.dram_tensor("cs", [M, 256], F32, kind="ExternalInput").ap()
    sn = nc.dram_tensor("sn", [M, 256], F32, kind="ExternalInput").ap()
    o = nc.dram_tensor("qkv", [M, N], F32, kind="ExternalOutput").ap()
    nK = K // 128
    st = ExitStack()
    with st:
        asb = st.enter_context(nc.sbuf_tensor("asb", [128, nK * M], BF16))
        wst = st.enter_context(nc.sbuf_tensor("wst", [128, 512], F32))
        wsb = st.enter_context(nc.sbuf_tensor("wsb", [128, 512], BF16))
        ps = st.enter_context(nc.psum_tensor("ps", [128, 512], F32))
        ch = st.enter_context(nc.sbuf_tensor("ch", [128, 512], F32))
        och = st.enter_context(nc.sbuf_tensor("och", [128, 512], F32))
        ct = st.enter_context(nc.sbuf_tensor("ct", [128, 256], F32))
        sts = st.enter_context(nc.sbuf_tensor("sts", [128, 256], F32))
        t1 = st.enter_context(nc.sbuf_tensor("t1", [128, 256], F32))
        t2 = st.enter_context(nc.sbuf_tensor("t2", [128, 256], F32))
        t3 = st.enter_context(nc.sbuf_tensor("t3", [128, 256], F32))
        t4 = st.enter_context(nc.sbuf_tensor("t4", [128, 256], F32))
        q = Seq()
        for kt in range(nK):
            q.dma(asb[:, kt * M:(kt + 1) * M], at[kt * 128:(kt + 1) * 128, :])
        for mt in range(M // 128):
            q.dma(ct[:, :], cs[mt * 128:(mt + 1) * 128, :])
            q.dma(sts[:, :], sn[mt * 128:(mt + 1) * 128, :])
            for nt in range(N // 512):
                for kt in range(nK):
                    q.dma(wst[:, :], w[kt * 128:(kt + 1) * 128, nt * 512:(nt + 1) * 512])
                    q.s("scalar", lambda e: e.copy(wsb[:, :], wst[:, :]))
                    q.s("tensor", lambda e, kt=kt, mt=mt: e.matmul(
                        ps[:, :], asb[:, kt * M + mt * 128:kt * M + mt * 128 + 128],
                        wsb[:, :], start=(kt == 0), stop=(kt == nK - 1)))
                q.s("scalar", lambda e: e.copy(ch[:, :], ps[:, :]))
                if nt < 8:
                    x1 = ch[:, :].rearrange("p (h t d) -> p h t d", h=4, t=2, d=64)
                    o1 = och[:, :].rearrange("p (h t d) -> p h t d", h=4, t=2, d=64)
                    cv = ct[:, :].rearrange("p (h d) -> p h d", h=4, d=64)
                    sv = sts[:, :].rearrange("p (h d) -> p h d", h=4, d=64)
                    t1v = t1[:, :].rearrange("p (h d) -> p h d", h=4, d=64)
                    t2v = t2[:, :].rearrange("p (h d) -> p h d", h=4, d=64)
                    t3v = t3[:, :].rearrange("p (h d) -> p h d", h=4, d=64)
                    t4v = t4[:, :].rearrange("p (h d) -> p h d", h=4, d=64)
                    q.s("vector", lambda e, x1=x1, cv=cv, t1v=t1v: e.tensor_mul(t1v, x1[:, :, 0, :], cv))
                    q.s("vector", lambda e, x1=x1, sv=sv, t2v=t2v: e.tensor_mul(t2v, x1[:, :, 1, :], sv))
                    q.s("vector", lambda e, x1=x1, sv=sv, t3v=t3v: e.tensor_mul(t3v, x1[:, :, 0, :], sv))
                    q.s("vector", lambda e, x1=x1, cv=cv, t4v=t4v: e.tensor_mul(t4v, x1[:, :, 1, :], cv))
                    q.s("vector", lambda e, o1=o1, t1v=t1v, t2v=t2v: e.tensor_tensor(o1[:, :, 0, :], t1v, t2v, OP.subtract))
                    q.s("vector", lambda e, o1=o1, t3v=t3v, t4v=t4v: e.tensor_tensor(o1[:, :, 1, :], t3v, t4v, OP.add))
                    q.dma(o[mt * 128:(mt + 1) * 128, nt * 512:(nt + 1) * 512], och[:, :])
                else:
                    q.dma(o[mt * 128:(mt + 1) * 128, nt * 512:(nt + 1) * 512], ch[:, :])
        q.run(nc, st)
    return nc


def build_attn(nbh=4, T=4096, d=128):
    nc = bass.Bass("TRN2", target_bir_lowering=False)
    qT = nc.dram_tensor("qT", [nbh * 128, T], F32, kind="ExternalInput").ap()
    kT = nc.dram_tensor("kT", [nbh * 128, T], F32, kind="ExternalInput").ap()
    vv = nc.dram_tensor("vv", [nbh * T, d], F32, kind="ExternalInput").ap()
    ngm = nc.dram_tensor("ngm", [128, 128], F32, kind="ExternalInput").ap()
    idn = nc.dram_tensor("idn", [128, 128], BF16, kind="ExternalInput").ap()
    ao = nc.dram_tensor("ao", [nbh * T, d], F32, kind="ExternalOutput").ap()
    scl = 1.0 / np.sqrt(128.0)
    st = ExitStack()
    with st:
        qst = st.enter_context(nc.sbuf_tensor("qst", [128, T], F32))
        qb = st.enter_context(nc.sbuf_tensor("qb", [128, T], BF16))
        kb = st.enter_context(nc.sbuf_tensor("kb", [128, T], BF16))
        vsb = st.enter_context(nc.sbuf_tensor("vsb", [128, T], F32))
        vb = st.enter_context(nc.sbuf_tensor("vb", [128, T], BF16))
        ngt = st.enter_context(nc.sbuf_tensor("ngt", [128, 128], F32))
        idt = st.enter_context(nc.sbuf_tensor("idt", [128, 128], BF16))
        Ssb = st.enter_context(nc.sbuf_tensor("Ssb", [128, T], F32))
        Esb = st.enter_context(nc.sbuf_tensor("Esb", [128, T], F32))
        Psb = st.enter_context(nc.sbuf_tensor("Psb", [128, T], BF16))
        PTs = st.enter_context(nc.sbuf_tensor("PTs", [128, T], BF16))
        mx = st.enter_context(nc.sbuf_tensor("mx", [128, 1], F32))
        nmx = st.enter_context(nc.sbuf_tensor("nmx", [128, 1], F32))
        sm = st.enter_context(nc.sbuf_tensor("sm", [128, 1], F32))
        rc = st.enter_context(nc.sbuf_tensor("rc", [128, 1], F32))
        ot = st.enter_context(nc.sbuf_tensor("ot", [128, d], F32))
        sps = st.enter_context(nc.psum_tensor("sps", [128, 512], F32))
        tps = st.enter_context(nc.psum_tensor("tps", [128, 128], BF16))
        ops = st.enter_context(nc.psum_tensor("ops", [128, d], F32))
        q = Seq()
        q.dma(ngt[:, :], ngm[:, :])
        q.dma(idt[:, :], idn[:, :])
        for bh in range(nbh):
            q.dma(qst[:, :], qT[bh * 128:(bh + 1) * 128, :])
            q.s("scalar", lambda e: e.copy(qb[:, :], qst[:, :]))
            q.dma(qst[:, :], kT[bh * 128:(bh + 1) * 128, :])
            q.s("scalar", lambda e: e.copy(kb[:, :], qst[:, :]))
            q.dma(vsb[:, :], vv[bh * T:(bh + 1) * T, :].rearrange("(t p) d -> p t d", p=128))
            q.s("scalar", lambda e: e.copy(vb[:, :], vsb[:, :]))
            for qi in range(T // 128):
                L = (qi + 1) * 128
                nch = (L + 511) // 512
                for kc in range(nch):
                    nw = min(512, L - kc * 512)
                    q.s("tensor", lambda e, qi=qi, kc=kc, nw=nw: e.matmul(
                        sps[:, :nw], qb[:, qi * 128:(qi + 1) * 128], kb[:, kc * 512:kc * 512 + nw],
                        start=True, stop=True))
                    if kc == nch - 1:
                        off = qi * 128 - kc * 512
                        if off > 0:
                            q.s("scalar", lambda e, kc=kc, off=off: e.mul(
                                Ssb[:, kc * 512:kc * 512 + off], sps[:, :off], scl))
                        q.s("vector", lambda e, qi=qi, off=off: e.scalar_tensor_tensor(
                            Ssb[:, qi * 128:(qi + 1) * 128], sps[:, off:off + 128], scl,
                            ngt[:, :], OP.mult, OP.add))
                    else:
                        q.s("scalar", lambda e, kc=kc, nw=nw: e.mul(
                            Ssb[:, kc * 512:kc * 512 + nw], sps[:, :nw], scl))
                q.s("vector", lambda e, L=L: e.reduce_max(mx[:, 0:1], Ssb[:, :L], AX))
                q.s("scalar", lambda e: e.mul(nmx[:, 0:1], mx[:, 0:1], -1.0))
                q.s("scalar", lambda e, L=L: e.activation(Esb[:, :L], Ssb[:, :L], AF.Exp, bias=nmx[:, 0:1]))
                q.s("vector", lambda e, L=L: e.reduce_sum(sm[:, 0:1], Esb[:, :L], AX))
                q.s("vector", lambda e: e.reciprocal(rc[:, 0:1], sm[:, 0:1]))
                q.s("vector", lambda e, L=L: e.tensor_scalar(Psb[:, :L], Esb[:, :L], rc[:, 0:1], None, OP.mult))
                for kt in range(qi + 1):
                    q.s("tensor", lambda e, kt=kt: e.transpose(tps[:, :], Psb[:, kt * 128:(kt + 1) * 128], idt[:, :]))
                    q.s("scalar", lambda e, kt=kt: e.copy(PTs[:, kt * 128:(kt + 1) * 128], tps[:, :]))
                for kt in range(qi + 1):
                    q.s("tensor", lambda e, kt=kt, qi=qi: e.matmul(
                        ops[:, :], PTs[:, kt * 128:(kt + 1) * 128], vb[:, kt * 128:(kt + 1) * 128],
                        start=(kt == 0), stop=(kt == qi)))
                q.s("scalar", lambda e: e.copy(ot[:, :], ops[:, :]))
                q.dma(ao[bh * T + qi * 128:bh * T + (qi + 1) * 128, :], ot[:, :])
        q.run(nc, st)
    return nc


def _runs(nc, in_maps):
    res = run_bass_kernel_spmd(nc, in_maps, core_ids=list(range(NCORE)))
    return res.results


def _c(a):
    return np.ascontiguousarray(a)


def kernel(L0_attn_x, L0_attn_in_proj_w, L0_attn_conv_w, L0_attn_conv_b,
           L0_attn_dt_bias, L0_attn_A_log, L0_attn_D, L0_attn_norm_w,
           L0_attn_out_proj_w, L0_ffn_lut, L0_ffn_gate_idx, L0_ffn_up_idx,
           L0_ffn_down_idx, L1_attn_Wq, L1_attn_Wk, L1_attn_Wv, L1_attn_Wo,
           L1_ffn_w_gate, L1_ffn_w_up, L1_ffn_w_down):
    TOK = 8192
    MS = TOK // NCORE
    x0 = np.asarray(L0_attn_x, np.float32).reshape(TOK, 2048)

    nc = build_ln(MS, 2048)
    rs = _runs(nc, [{"x": _c(x0[i * MS:(i + 1) * MS])} for i in range(NCORE)])
    ln1 = np.concatenate([r["y"] for r in rs], 0)

    ln1T = _c(ln1.T)
    wip = _c(np.asarray(L0_attn_in_proj_w, np.float32).T)
    nc = build_mm_v1(2048, MS, [512] * 20 + [32], ["w"], "proj", 10272)
    rs = _runs(nc, [{"at": _c(ln1T[:, i * MS:(i + 1) * MS]), "w": wip} for i in range(NCORE)])
    proj = np.concatenate([r["proj"] for r in rs], 0)
    gate = proj[:, :4096]
    hbc = proj[:, 4096:10240]
    dtr = proj[:, 10240:]

    hbcT = _c(hbc.T)
    cw = np.asarray(L0_attn_conv_w, np.float32).reshape(6144, 4)
    cb = np.asarray(L0_attn_conv_b, np.float32).reshape(6144, 1)
    nc = build_conv(768, TOK, 4096)
    rs = _runs(nc, [{"x": _c(hbcT[i * 768:(i + 1) * 768]), "cw": _c(cw[i * 768:(i + 1) * 768]),
                     "cb": _c(cb[i * 768:(i + 1) * 768])} for i in range(NCORE)])
    hbcS = np.concatenate([r["y"] for r in rs], 0)

    dtr4 = dtr.reshape(2, 16, 256, 32).transpose(0, 3, 1, 2).reshape(64, 16, 256)
    dtb = np.asarray(L0_attn_dt_bias, np.float32)
    alog = np.asarray(L0_attn_A_log, np.float32)
    hidx = np.arange(64) % 32
    nc = build_dtprep()
    ims = []
    for i in range(NCORE):
        rows = dtr4[i * 8:(i + 1) * 8].reshape(128, 256)
        hh = np.repeat(hidx[i * 8:(i + 1) * 8], 16)
        ims.append({"dtr": _c(rows), "dtb": _c(dtb[hh][:, None]), "alog": _c(alog[hh][:, None])})
    rs = _runs(nc, ims)
    dt2 = np.stack([r["dt2"] for r in rs], 0).reshape(64, 16, 256)
    acum = np.stack([r["acum"] for r in rs], 0).reshape(64, 16, 256)

    def tposeT(a):
        return _c(a.reshape(64, 16, 2, 128).transpose(0, 3, 1, 2).reshape(64, 128, 32))
    acT = tposeT(acum)
    dtT = tposeT(dt2)
    abt = _c(np.broadcast_to(acum[:, :, 255][:, None, :], (64, 128, 16)))
    D = np.asarray(L0_attn_D, np.float32)
    dcol = _c(np.broadcast_to(D[hidx][:, None, None], (64, 128, 1)))
    mut = np.zeros((256, 256), np.float32)
    for sh_ in range(2):
        for p in range(128):
            mut[sh_ * 128 + p, sh_ * 128 + p:] = 1.0
    ones1 = np.ones((1, 128), np.float32)
    gidx = hidx % 8
    hsx_all = np.empty((64, 4096, 128), np.float32)
    bln_all = np.empty((64, 4096, 128), np.float32)
    btr_all = np.empty((64, 128, 4096), np.float32)
    ctr_all = np.empty((64, 128, 4096), np.float32)
    for bh in range(64):
        b, h = bh // 32, bh % 32
        g = h % 8
        hsx_all[bh] = hbcS[h * 128:(h + 1) * 128, b * 4096:(b + 1) * 4096].T
        bb = hbcS[4096 + g * 128:4096 + (g + 1) * 128, b * 4096:(b + 1) * 4096]
        cc = hbcS[5120 + g * 128:5120 + (g + 1) * 128, b * 4096:(b + 1) * 4096]
        bln_all[bh] = bb.T
        btr_all[bh] = bb
        ctr_all[bh] = cc
    nc = build_scan()
    ims = []
    for i in range(NCORE):
        sl = slice(i * 8, (i + 1) * 8)
        ims.append({
            "hsx": _c(hsx_all[sl].reshape(32768, 128)),
            "bln": _c(bln_all[sl].reshape(32768, 128)),
            "btr": _c(btr_all[sl].reshape(1024, 4096)),
            "ctr": _c(ctr_all[sl].reshape(1024, 4096)),
            "acT": _c(acT[sl].reshape(1024, 32)),
            "dtT": _c(dtT[sl].reshape(1024, 32)),
            "abt": _c(abt[sl].reshape(1024, 16)),
            "dcol": _c(dcol[sl].reshape(1024, 1)),
            "arows": _c(acum[sl].reshape(128, 256)),
            "mut": mut, "ones1": ones1,
        })
    rs = _runs(nc, ims)
    y_all = np.concatenate([r["yout"] for r in rs], 0).reshape(64, 4096, 128)
    y_tok = _c(y_all.reshape(2, 32, 4096, 128).transpose(0, 2, 1, 3).reshape(TOK, 4096))

    nw_b = _c(np.broadcast_to(np.asarray(L0_attn_norm_w, np.float32)[None, :], (128, 4096)))
    nc = build_gating(MS, 4096, 512)
    rs = _runs(nc, [{"yy": _c(y_tok[i * MS:(i + 1) * MS]), "gg": _c(gate[i * MS:(i + 1) * MS]),
                     "nw": nw_b} for i in range(NCORE)])
    hgt = np.concatenate([r["hh"] for r in rs], 0)

    hgtT = _c(hgt.T)
    wop = _c(np.asarray(L0_attn_out_proj_w, np.float32).T)
    nc = build_mm_v2(4096, MS, 2048, "x1", "ln2")
    rs = _runs(nc, [{"at": _c(hgtT[:, i * MS:(i + 1) * MS]), "w": wop,
                     "res": _c(x0[i * MS:(i + 1) * MS])} for i in range(NCORE)])
    x1 = np.concatenate([r["x1"] for r in rs], 0)
    ln2 = np.concatenate([r["ln2"] for r in rs], 0)

    ln2T = _c(ln2.T)
    gixT = _c(np.asarray(L0_ffn_gate_idx, np.int32).T)
    uixT = _c(np.asarray(L0_ffn_up_idx, np.int32).T)
    dixT = _c(np.asarray(L0_ffn_down_idx, np.int32).T)
    lut_b = _c(np.broadcast_to(np.asarray(L0_ffn_lut, np.float32)[None, :], (128, 16)))
    nc = build_ffn0()
    rs = _runs(nc, [{"at": ln2T, "gix": _c(gixT[:, i * 688:(i + 1) * 688]),
                     "uix": _c(uixT[:, i * 688:(i + 1) * 688]),
                     "dix": _c(dixT[:, i * 256:(i + 1) * 256]), "lut": lut_b}
                    for i in range(NCORE)])
    hff = np.concatenate([r["hf"] for r in rs], 1)
    wdb = np.concatenate([r["wd"] for r in rs], 1)

    hffT = _c(hff.T)
    nc = build_mm_v2(5504, MS, 2048, "x2", "ln3", w_bf=True)
    rs = _runs(nc, [{"at": _c(hffT[:, i * MS:(i + 1) * MS]), "w": wdb,
                     "res": _c(x1[i * MS:(i + 1) * MS])} for i in range(NCORE)])
    x2 = np.concatenate([r["x2"] for r in rs], 0)
    ln3 = np.concatenate([r["ln3"] for r in rs], 0)

    ln3T = _c(ln3.T)
    wqkv = _c(np.concatenate([np.asarray(L1_attn_Wq, np.float32).T,
                              np.asarray(L1_attn_Wk, np.float32).T,
                              np.asarray(L1_attn_Wv, np.float32).T], 1))
    invf = 1.0 / 10000.0 ** (np.arange(64, dtype=np.float64) / 64.0)
    nc = build_qkv()
    ims = []
    for i in range(NCORE):
        pos = (np.arange(i * MS, (i + 1) * MS) % 4096).astype(np.float64)
        ang = pos[:, None] * invf[None, :]
        c64 = np.cos(ang).astype(np.float32)
        s64 = np.sin(ang).astype(np.float32)
        ims.append({"at": _c(ln3T[:, i * MS:(i + 1) * MS]), "w": wqkv,
                    "cs": _c(np.tile(c64, (1, 4))), "sn": _c(np.tile(s64, (1, 4)))})
    rs = _runs(nc, ims)
    qkv = np.concatenate([r["qkv"] for r in rs], 0)

    qh = _c(qkv[:, :2048].reshape(2, 4096, 16, 128).transpose(0, 2, 3, 1).reshape(32, 128, 4096))
    kh = _c(qkv[:, 2048:4096].reshape(2, 4096, 16, 128).transpose(0, 2, 3, 1).reshape(32, 128, 4096))
    vh = _c(qkv[:, 4096:].reshape(2, 4096, 16, 128).transpose(0, 2, 1, 3).reshape(32, 4096, 128))
    ngm = np.zeros((128, 128), np.float32)
    for p in range(128):
        ngm[p, p + 1:] = -1e30
    idn = np.eye(128, dtype=BF)
    nc = build_attn()
    rs = _runs(nc, [{"qT": _c(qh[i * 4:(i + 1) * 4].reshape(512, 4096)),
                     "kT": _c(kh[i * 4:(i + 1) * 4].reshape(512, 4096)),
                     "vv": _c(vh[i * 4:(i + 1) * 4].reshape(16384, 128)),
                     "ngm": ngm, "idn": idn} for i in range(NCORE)])
    att = np.concatenate([r["ao"] for r in rs], 0).reshape(32, 4096, 128)
    att_tok = _c(att.reshape(2, 16, 4096, 128).transpose(0, 2, 1, 3).reshape(TOK, 2048))

    attT = _c(att_tok.T)
    wo = _c(np.asarray(L1_attn_Wo, np.float32).T)
    nc = build_mm_v2(2048, MS, 2048, "x3", "ln4", a_bf=False)
    rs = _runs(nc, [{"at": _c(attT[:, i * MS:(i + 1) * MS]), "w": wo,
                     "res": _c(x2[i * MS:(i + 1) * MS])} for i in range(NCORE)])
    x3 = np.concatenate([r["x3"] for r in rs], 0)
    ln4 = np.concatenate([r["ln4"] for r in rs], 0)

    ln4T = _c(ln4.T)
    wg1 = _c(np.asarray(L1_ffn_w_gate, np.float32).T)
    wu1 = _c(np.asarray(L1_ffn_w_up, np.float32).T)
    nc = build_mm_v1(2048, MS, [512] * 16, ["wg", "wu"], "h1", 8192, dual=True, out_dtype=BF16)
    rs = _runs(nc, [{"at": _c(ln4T[:, i * MS:(i + 1) * MS]), "wg": wg1, "wu": wu1}
                    for i in range(NCORE)])
    h1 = np.concatenate([r["h1"] for r in rs], 0)

    h1T = _c(h1.T)
    wd1 = _c(np.asarray(L1_ffn_w_down, np.float32).T)
    nc = build_mm_v1(8192, MS, [512] * 4, ["w"], "x4", 2048, res=True)
    rs = _runs(nc, [{"at": _c(h1T[:, i * MS:(i + 1) * MS]), "w": wd1,
                     "res": _c(x3[i * MS:(i + 1) * MS])} for i in range(NCORE)])
    x4 = np.concatenate([r["x4"] for r in rs], 0)
    return x4.reshape(2, 4096, 2048)
